# version 03
# baseline (speedup 1.0000x reference)
import numpy as np
from contextlib import ExitStack
import concourse.bass as bass
import concourse.mybir as mybir
from concourse.bass_utils import run_bass_kernel_spmd

F32 = mybir.dt.float32
BF16 = mybir.dt.bfloat16
AF = mybir.ActivationFunctionType
ALU = mybir.AluOpType
AX = mybir.AxisListType.X

NSEQ, T, C, O = 2, 2048, 2048, 8192
NCH = C // 128
NOC = O // 128
TC = 512
NTT = TC // 128
NCHUNK = (NSEQ * T) // TC
NHB = 16
NOB = 16
SEG = 256

ENGINES = ("sync", "gpsimd", "vector", "scalar", "tensor")


class Sched:
    def __init__(self):
        self.ops = {e: [] for e in ENGINES}
        self.cnt = {}

    def sem(self, name):
        self.cnt.setdefault(name, 0)

    def wait(self, eng, sname, val):
        if val and val > 0:
            self.ops[eng].append(("w", sname, val))

    def do(self, eng, fn, inc=None, n=1):
        if inc is not None:
            self.cnt[inc] += n
        self.ops[eng].append(("i", fn, inc, n))
        return self.cnt[inc] if inc is not None else None

    def val(self, sname):
        return self.cnt[sname]

    def verify(self):
        ptr = {e: 0 for e in ENGINES}
        cnt = {k: 0 for k in self.cnt}
        progress = True
        while progress:
            progress = False
            for e_ in ENGINES:
                while ptr[e_] < len(self.ops[e_]):
                    op = self.ops[e_][ptr[e_]]
                    if op[0] == "w":
                        if cnt[op[1]] < op[2]:
                            break
                    else:
                        if op[2] is not None:
                            cnt[op[2]] += op[3]
                    ptr[e_] += 1
                    progress = True
        stuck = {e_: (ptr[e_], len(self.ops[e_])) for e_ in ENGINES
                 if ptr[e_] < len(self.ops[e_])}
        if stuck:
            detail = {}
            for e_, (p, n) in stuck.items():
                op = self.ops[e_][p]
                detail[e_] = f"op {p}/{n} waiting {op[1]} >= {op[2]} (now {cnt[op[1]]})"
            raise RuntimeError(f"schedule deadlock: {detail}")

    def replay(self, eng_name, E, H):
        for op in self.ops[eng_name]:
            if op[0] == "w":
                E.wait_ge(H[op[1]], op[2])
            else:
                _, fn, inc, n = op
                ins = fn(E)
                if inc is not None:
                    ins.then_inc(H[inc], n)


def build(dumps=None):
    dumps = dumps or set()
    nc = bass.Bass("TRN2", target_bir_lowering=False, debug=False)

    x_in = nc.dram_tensor("x", [NSEQ, T, C], F32, kind="ExternalInput").ap()
    out_t = nc.dram_tensor("out", [NSEQ, T, C], F32, kind="ExternalOutput").ap()
    W = {w: nc.dram_tensor("W" + w, [C, C], F32, kind="ExternalInput").ap()
         for w in "rkvo"}
    W_fc = nc.dram_tensor("Wfc", [O, C], F32, kind="ExternalInput").ap()
    W_pr = nc.dram_tensor("Wpr", [C, O], F32, kind="ExternalInput").ap()
    mu_in = {w: nc.dram_tensor("mu" + w, [C], F32, kind="ExternalInput").ap()
             for w in "rkv"}
    wkvw = nc.dram_tensor("wkvw", [C], F32, kind="ExternalInput").ap()
    wkvu = nc.dram_tensor("wkvu", [C], F32, kind="ExternalInput").ap()
    ident_in = nc.dram_tensor("identd", [128, 128], F32, kind="ExternalInput").ap()

    wbf = {w: nc.dram_tensor(f"w{w}bf", [C, C], BF16, kind="Internal").ap()
           for w in "kvro"}
    wkvrT = nc.dram_tensor("wkvrT", [C, NHB, 3, 128], BF16, kind="Internal").ap()
    woT = nc.dram_tensor("woT", [C, C], BF16, kind="Internal").ap()
    tfc = nc.dram_tensor("tfc", [O, C], BF16, kind="Internal").ap()
    fcT = nc.dram_tensor("fcT", [C, O], BF16, kind="Internal").ap()
    tpr = nc.dram_tensor("tpr", [C, O], BF16, kind="Internal").ap()
    prT = nc.dram_tensor("prT", [O, C], BF16, kind="Internal").ap()
    xhat_d = nc.dram_tensor("xhatd", [TC, C], BF16, kind="Internal").ap()
    xh2_d = nc.dram_tensor("xh2d", [TC, O], BF16, kind="Internal").ap()

    dump_t = {}
    def add_dump(name, shape, dtype):
        if name in dumps:
            dump_t[name] = nc.dram_tensor("dbg_" + name, shape, dtype,
                                          kind="ExternalOutput").ap()
    add_dump("xn_fm", [128, NCH, TC + 2], F32)
    add_dump("stats", [128, 120], F32)
    add_dump("z_ch", [128, NHB, TC], BF16)
    add_dump("s_tm", [128, NTT, C], F32)
    add_dump("h0", [128, NTT, C], BF16)
    add_dump("fstat", [128, 4 * 80], F32)
    add_dump("gam", [128, 8], F32)
    add_dump("xi3", [128, 3, NCH, TC], BF16)

    S = Sched()
    for s in ("gq", "sq", "pe", "dve", "act", "pq"):
        S.sem(s)

    with ExitStack() as ctx:
        e = ctx.enter_context
        decay = e(nc.sbuf_tensor("decay", [128, NHB, SEG], F32))
        v_mu = {w: e(nc.sbuf_tensor("v_mu" + w, [128, NCH], F32)) for w in "rkv"}
        v_d = e(nc.sbuf_tensor("v_d", [128, NCH], F32))
        v_eu = e(nc.sbuf_tensor("v_eu", [128, NCH], F32))
        v_tmp = e(nc.sbuf_tensor("v_tmp", [128, NCH], F32))
        aacar = e(nc.sbuf_tensor("aacar", [128, NHB], F32))
        bbcar = e(nc.sbuf_tensor("bbcar", [128, NHB], F32))
        xncar = e(nc.sbuf_tensor("xncar", [128, NCH], F32))
        ident = e(nc.sbuf_tensor("ident", [128, 128], F32))
        ones = e(nc.sbuf_tensor("ones", [128, 128], F32))
        stats = e(nc.sbuf_tensor("stats", [128, 120], F32))
        fstat = e(nc.sbuf_tensor("fstat", [128, 4, 80], F32))
        gam = e(nc.sbuf_tensor("gam", [128, 8], F32))
        outst = e(nc.sbuf_tensor("outst", [128, 2, TC], F32))
        gbuf = e(nc.sbuf_tensor("gbuf", [128, 2, TC], F32))
        xjit = e(nc.sbuf_tensor("xjit", [128, 2, TC], F32))
        xn_fm = e(nc.sbuf_tensor("xn_fm", [128, NCH, TC + 2], F32))
        xi3 = e(nc.sbuf_tensor("xi3", [128, 3, NCH, TC], BF16))
        z_ch = e(nc.sbuf_tensor("z_ch", [128, NHB, TC], BF16))
        big = e(nc.sbuf_tensor("big", [128, 8192], F32))
        xh_sl = e(nc.sbuf_tensor("xh_sl", [128, 8192], BF16))
        wstr = e(nc.sbuf_tensor("wstr", [128, 2, 8, 3, 128], BF16))
        wstr2 = e(nc.sbuf_tensor("wstr2", [128, 2, 8, 512], BF16))
        prep_w = e(nc.sbuf_tensor("prep_w", [128, 2, 512], F32))
        pb = [e(nc.psum_tensor(f"pb{i}", [128, 512], F32)) for i in range(8)]

        bigf = big.ap()
        xT_v = bigf.rearrange("p (a b) -> p a b", a=4)
        s_tm = xT_v
        xi3_flat = xi3.ap().rearrange("p a b c -> p (a b c)")
        z_flat = z_ch.ap().rearrange("p a b -> p (a b)")
        z_f32 = z_flat.bitcast(F32)
        xnf_bf = xn_fm.ap().rearrange("p a b -> p (a b)").bitcast(BF16)
        hslabs = [
            xi3_flat[:, 0:8192].rearrange("p (a b) -> p a b", a=NTT),
            z_flat[:, 0:8192].rearrange("p (a b) -> p a b", a=NTT),
            xnf_bf[:, 0:8192].rearrange("p (a b) -> p a b", a=NTT),
            xnf_bf[:, 8192:16384].rearrange("p (a b) -> p a b", a=NTT),
        ]
        xh2fm = [sl.rearrange("p a b -> p (a b)").rearrange("p (a b) -> p a b", a=NCH)
                 for sl in hslabs]
        xhat_fm = xh_sl.ap().rearrange("p (a b) -> p a b", a=NCH)
        xhat_st = xh_sl.ap().rearrange("p (a b) -> p a b", a=4)
        ptrash32 = z_f32[:, 0:2048]
        gbuf_flat = gbuf.ap().rearrange("p a b -> p (a b)")
        ftrash = gbuf_flat[:, 0:512]
        gbuf_bf = gbuf_flat.bitcast(BF16)
        xjit_bf = xjit.ap().rearrange("p a b -> p (a b)").bitcast(BF16)
        prep_s = [gbuf_bf[:, 0:2048], xjit_bf[:, 0:2048]]
        outst_bf = outst.ap().rearrange("p a b -> p (a b)").bitcast(BF16)
        prep_o = [outst_bf[:, 1024:1536], outst_bf[:, 1536:2048]]

        H = {}
        for s in ("gq", "sq", "pe", "dve", "act", "pq"):
            H[s] = e(nc.semaphore(s))

        def dma(eng, out, in_, **kw):
            return S.do(eng, lambda E, o=out, i=in_, k=kw: E.dma_start(out=o, in_=i, **k),
                        inc=("gq" if eng == "gpsimd" else "sq"), n=16)

        def dmaT(out, in_):
            return S.do("sync", lambda E, o=out, i=in_: E.dma_start(out=o, in_=i, transpose=True),
                        inc="sq", n=16)

        ST = {}
        for i, nm in enumerate(("sm", "sq", "mx", "mn", "m1", "msq", "mm", "tmx",
                                "tmn", "vare", "lnv", "lnvb", "is1", "t1", "i2",
                                "rd", "varr", "v2", "cl", "v2e", "ncl", "nb1",
                                "lnv2", "lnv2b", "is2", "ps1")):
            ST[nm] = i * 4

        def stat(nm, tt):
            i = ST[nm] + tt
            return stats[:, i:i + 1]

        def statw(nm):
            return stats[:, ST[nm]:ST[nm] + 4]

        FB = {"hsum": 0, "hsq": 16, "hmx": 32, "hmn": 48}

        def fst(nm, tt, ob=None):
            b = FB[nm]
            if ob is None:
                return fstat[:, tt, b:b + 16]
            return fstat[:, tt, b + ob:b + ob + 1]

        def ffin(i, tt):
            return fstat[:, tt, 64 + i:65 + i]

        for w in "rkv":
            dma("gpsimd", v_mu[w][:], mu_in[w].rearrange("(a p) -> p a", p=128),
                allow_slow_non_contiguous=True)
        dma("gpsimd", v_d[:], wkvw.rearrange("(a p) -> p a", p=128),
            allow_slow_non_contiguous=True)
        dma("gpsimd", v_eu[:], wkvu.rearrange("(a p) -> p a", p=128),
            allow_slow_non_contiguous=True)
        cap_sm = S.val("gq")
        dma("gpsimd", ident[:], ident_in)
        cap_id = S.val("gq")
        S.wait("scalar", "gq", cap_sm)
        S.do("scalar", lambda E: E.activation(v_tmp[:], v_d[:], AF.Exp))
        S.do("scalar", lambda E: E.activation(v_d[:], v_tmp[:], AF.Exp, scale=-1.0))
        S.do("scalar", lambda E: E.activation(v_eu[:], v_eu[:], AF.Exp), inc="act")
        cap_vec = S.val("act")
        S.wait("vector", "act", cap_vec)
        ones256 = z_f32[:, 0:256]
        S.do("vector", lambda E: E.memset(ones256, 1.0))
        S.do("vector", lambda E: E.memset(ones[:], 1.0))
        for hb in range(NHB):
            S.do("vector", lambda E, hb=hb: E.tensor_scalar_mul(
                decay[:, hb, :], ones256, v_d[:, hb:hb + 1]))
        S.do("vector", lambda E: E.memset(aacar[:], 0.0))
        S.do("vector", lambda E: E.memset(bbcar[:], 0.0), inc="dve")
        cap_decay = S.val("dve")

        wfc_t = W_fc.rearrange("(a p) d -> a p d", p=128)
        wpr_t = W_pr.rearrange("(a p) o -> a p o", p=128)
        outst_flat = outst.ap().rearrange("p a b -> p (a b)")

        def wtile_src(mat, i):
            if mat == "fc":
                return wfc_t[i // 4, :, (i % 4) * 512:(i % 4) * 512 + 512]
            return wpr_t[i // 16, :, (i % 16) * 512:(i % 16) * 512 + 512]

        cap_cast = {}
        cap_wprep = {}
        prep_state = {"stg_free": [0, 0], "pw_free": [0, 0], "po_free": [0, 0],
                      "gacc_init": 0}
        cap_thr = {}
        cap_tern = {}
        cap_wT = {}

        def emit_attn_prep():
            for w in "kvro":
                dma("gpsimd", wbf[w], W[w])
                cap_cast[w] = S.val("gq")
            stg_free = prep_state["stg_free"]
            for wi, w in enumerate("kvr"):
                for cc in range(NCH):
                    b = (wi * NCH + cc) % 2
                    S.wait("sync", "gq", max(cap_cast[w], stg_free[b]))
                    dmaT(prep_s[b], wbf[w][:, cc * 128:(cc + 1) * 128])
                    cap_rd = S.val("sq")
                    S.wait("gpsimd", "sq", cap_rd)
                    src_v = prep_s[b].rearrange("p (hb h) -> p hb h", h=128)
                    for q in range(4):
                        dma("gpsimd",
                            wkvrT[cc * 128:(cc + 1) * 128, q * 4:(q + 1) * 4, wi, :],
                            src_v[:, q * 4:(q + 1) * 4, :])
                    stg_free[b] = S.val("gq")
                cap_wprep[w] = S.val("gq")
            for hc in range(NCH):
                b = hc % 2
                S.wait("sync", "gq", max(cap_cast["o"], stg_free[b]))
                dmaT(prep_s[b], wbf["o"][:, hc * 128:(hc + 1) * 128])
                cap_rd = S.val("sq")
                S.wait("gpsimd", "sq", cap_rd)
                dma("gpsimd", woT[hc * 128:(hc + 1) * 128, :], prep_s[b])
                stg_free[b] = S.val("gq")
            cap_wprep["o"] = S.val("gq")

        def emit_gamma_loads(mat):
            acc_i = 6 if mat == "fc" else 7
            acc = gam[:, acc_i:acc_i + 1]
            acc2 = v_tmp[:, 8:10]
            ring = 6
            S.do("vector", lambda E, a=acc2: E.memset(a, 0.0), inc="dve")
            pw_free = prep_state["pw_free"]
            for i in range(256):
                b = i % 2
                S.wait("gpsimd", "dve", pw_free[b])
                dma("gpsimd", prep_w[:, b], wtile_src(mat, i))
                cap = S.val("gq")
                S.wait("vector", "gq", cap)
                S.do("vector", lambda E, i=i, b=b: E.tensor_reduce(
                    v_tmp[:, 10 + i % ring:11 + i % ring], prep_w[:, b], AX, ALU.add,
                    apply_absolute_value=True))
                if i >= ring:
                    j = i - ring
                    S.do("vector", lambda E, j=j: E.tensor_add(
                        v_tmp[:, 8 + j % 2:9 + j % 2], v_tmp[:, 8 + j % 2:9 + j % 2],
                        v_tmp[:, 10 + j % ring:11 + j % ring]))
                S.do("vector", lambda E: E.engine_nop(), inc="dve")
                pw_free[b] = S.val("dve")
            for j in range(256 - ring, 256):
                S.do("vector", lambda E, j=j: E.tensor_add(
                    v_tmp[:, 8 + j % 2:9 + j % 2], v_tmp[:, 8 + j % 2:9 + j % 2],
                    v_tmp[:, 10 + j % ring:11 + j % ring]))
            S.do("vector", lambda E: E.engine_nop(), inc="dve")
            return S.val("dve")

        def emit_gamma_final(mat, cap_acc):
            gi = 0 if mat == "fc" else 1
            bank = 6 + gi
            S.wait("tensor", "dve", cap_acc)
            S.wait("tensor", "gq", cap_id)
            S.wait("tensor", pbank_eng[bank], pbank_guard[bank])
            S.do("tensor", lambda E, bank=bank: E.matmul(
                pb[bank][0:1, 0:2], ones[:, 0:1], v_tmp[:, 8:10],
                start=True, stop=True), inc="pe")
            c_g1 = S.val("pe")
            S.wait("vector", "pe", c_g1)
            S.do("vector", lambda E, gi=gi, bank=bank: E.tensor_copy(
                stats[0:1, 106:108], pb[bank][0:1, 0:2]), inc="dve")
            c_g2 = S.val("dve")
            S.wait("tensor", "dve", c_g2)
            S.do("tensor", lambda E, gi=gi, bank=bank: E.matmul(
                pb[bank][:, 0:2], ones[0:1, :], stats[0:1, 106:108],
                start=True, stop=True), inc="pe")
            c_g3 = S.val("pe")
            S.wait("vector", "pe", c_g3)
            S.do("vector", lambda E, gi=gi, bank=bank: E.tensor_copy(
                v_tmp[:, 2:4], pb[bank][:, 0:2]), inc="dve")
            cap_c = S.val("dve")
            pbank_guard[bank] = cap_c
            pbank_eng[bank] = "dve"
            S.wait("scalar", "dve", cap_c)
            S.do("scalar", lambda E: E.activation(
                v_tmp[:, 0:1], v_tmp[:, 2:3], AF.Identity, scale=1.0 / 16777216.0))
            S.do("scalar", lambda E: E.activation(
                v_tmp[:, 1:2], v_tmp[:, 3:4], AF.Identity, scale=1.0 / 16777216.0),
                inc="act")
            cap_a0 = S.val("act")
            S.wait("vector", "act", cap_a0)
            S.do("vector", lambda E, gi=gi: E.tensor_add(
                gam[:, gi:gi + 1], v_tmp[:, 0:1], v_tmp[:, 1:2]), inc="dve")
            cap_add = S.val("dve")
            S.wait("scalar", "dve", cap_add)
            S.do("scalar", lambda E: E.activation(
                v_tmp[:, 2:3], v_tmp[:, 2:3], AF.Identity), inc="act")
            cap_a = S.val("act")
            S.wait("vector", "act", cap_a)
            S.do("vector", lambda E, gi=gi: E.tensor_scalar(
                v_tmp[:, 4 + 2 * gi:5 + 2 * gi], gam[:, gi:gi + 1], 1e-6, 0.5,
                ALU.add, ALU.mult))
            S.do("vector", lambda E, gi=gi: E.tensor_scalar(
                v_tmp[:, 5 + 2 * gi:6 + 2 * gi], gam[:, gi:gi + 1], 1e-6, -0.5,
                ALU.add, ALU.mult), inc="dve")
            cap_d = S.val("dve")
            S.wait("scalar", "dve", cap_d)
            S.do("scalar", lambda E, gi=gi: E.activation(
                gam[:, 2 + 2 * gi:4 + 2 * gi], v_tmp[:, 4 + 2 * gi:6 + 2 * gi],
                AF.Identity), inc="act")
            cap_thr[mat] = S.val("act")
            if "gam" in dumps and mat == "pr":
                S.wait("gpsimd", "act", cap_thr[mat])
                dma("gpsimd", dump_t["gam"], gam[:])

        def emit_tern(mat):
            thr_i = 2 if mat == "fc" else 4
            nthr_i = thr_i + 1
            dst = tfc if mat == "fc" else tpr
            dnat = dst.rearrange("(a p) d -> a p d", p=128) if mat == "fc" else                    dst.rearrange("(a p) o -> a p o", p=128)
            pw_free = prep_state["pw_free"]
            po_free = prep_state["po_free"]
            def tern_dst(i):
                if mat == "fc":
                    return dnat[i // 4, :, (i % 4) * 512:(i % 4) * 512 + 512]
                return dnat[i // 16, :, (i % 16) * 512:(i % 16) * 512 + 512]

            def emit_stt(j):
                bj = j % 2
                S.do("vector", lambda E, bj=bj, ti=thr_i: E.scalar_tensor_tensor(
                    prep_o[bj], prep_w[:, bj], gam[:, ti:ti + 1],
                    outst_bf[:, bj * 512:bj * 512 + 512], ALU.is_ge,
                    ALU.subtract), inc="dve")
                pw_free[bj] = S.val("dve")
                cap_t = S.val("dve")
                S.wait("gpsimd", "dve", cap_t)
                dma("gpsimd", tern_dst(j), prep_o[bj])
                po_free[bj] = S.val("gq")

            pend = None
            for i in range(256):
                b = i % 2
                S.wait("gpsimd", "dve", pw_free[b])
                S.wait("gpsimd", "gq", po_free[b])
                dma("gpsimd", prep_w[:, b], wtile_src(mat, i))
                cap = S.val("gq")
                S.wait("vector", "gq", cap)
                S.wait("vector", "act", cap_thr[mat])
                S.do("vector", lambda E, b=b, ni=nthr_i: E.tensor_scalar(
                    outst_bf[:, b * 512:b * 512 + 512], prep_w[:, b],
                    gam[:, ni:ni + 1], None, ALU.is_le))
                if pend is not None:
                    emit_stt(pend)
                pend = i
            S.do("vector", lambda E: E.tensor_copy(
                v_tmp[:, 10:16], v_tmp[:, 10:16]))
            emit_stt(pend)
            cap_tern[mat] = S.val("gq")

        def emit_wT(mat):
            stg_free = prep_state["stg_free"]
            if mat == "fc":
                for si, (dc, q) in enumerate([(d, q) for d in range(NCH) for q in range(4)]):
                    b = si % 2
                    S.wait("sync", "gq", max(cap_tern["fc"], stg_free[b]))
                    dmaT(prep_s[b], tfc[q * 2048:(q + 1) * 2048, dc * 128:(dc + 1) * 128])
                    cap_rd = S.val("sq")
                    S.wait("gpsimd", "sq", cap_rd)
                    dma("gpsimd", fcT[dc * 128:(dc + 1) * 128, q * 2048:(q + 1) * 2048],
                        prep_s[b])
                    stg_free[b] = S.val("gq")
                cap_wT["fc"] = S.val("gq")
            else:
                for oc in range(NOC):
                    b = oc % 2
                    S.wait("sync", "gq", max(cap_tern["pr"], stg_free[b]))
                    dmaT(prep_s[b], tpr[:, oc * 128:(oc + 1) * 128])
                    cap_rd = S.val("sq")
                    S.wait("gpsimd", "sq", cap_rd)
                    dma("gpsimd", prT[oc * 128:(oc + 1) * 128, :], prep_s[b])
                    stg_free[b] = S.val("gq")
                cap_wT["pr"] = S.val("gq")

        pbank_guard = [0] * 8
        pbank_eng = ["act"] * 8

        s_done_prev = 0
        z_free_prev = 0
        xh2w_prev = 0
        xhfm_rd_prev = 0
        hslab_free_prev = 0
        wstr_free = [0, 0]
        wstr2_free = [0, 0]
        carry_cap = cap_decay
        carry_cap_act = 0
        dbg_gq = 0

        wk_src = wkvrT.rearrange("(cc p) hb w h -> p cc hb w h", p=128)
        woT_v = woT.rearrange("(hc p) c -> p hc c", p=128)
        fcT_v = fcT.rearrange("(dc p) o -> p dc o", p=128)
        prT_v = prT.rearrange("(oc p) d -> p oc d", p=128)

        for ch in range(NCHUNK):
            seq, t0 = ch // 4, (ch % 4) * TC
            first = (ch % 4) == 0

            S.wait("vector", "dve", s_done_prev)
            S.wait("vector", "pe", hslab_free_prev)
            if first:
                S.do("vector", lambda E: E.memset(xn_fm[:, :, 0:1], 0.0), inc="dve")
            else:
                S.do("vector", lambda E: E.tensor_copy(
                    xn_fm[:, :, 0:1], xncar[:].rearrange("p (a o) -> p a o", o=1)), inc="dve")
            cap_car_in = S.val("dve")

            xh_w = 0
            zxn2 = [z_f32[:, 0:2048], z_f32[:, 2048:4096]]
            for tt in range(NTT):
                xT = xT_v[:, tt]
                S.wait("gpsimd", "dve", s_done_prev)
                S.wait("gpsimd", "gq", dbg_gq)
                dma("gpsimd", xT, x_in[seq, t0 + tt * 128:t0 + (tt + 1) * 128, :])
                cap_x = S.val("gq")
                S.wait("vector", "gq", cap_x)
                S.do("vector", lambda E, xT=xT, tt=tt: E.tensor_reduce(
                    stat("sm", tt), xT, AX, ALU.add))
                S.do("vector", lambda E, xT=xT, tt=tt: E.tensor_reduce(
                    stat("mx", tt), xT, AX, ALU.max))
                S.do("vector", lambda E, xT=xT, tt=tt: E.tensor_reduce(
                    stat("mn", tt), xT, AX, ALU.min), inc="dve")
                S.wait("scalar", "gq", cap_x)
                S.wait("scalar", "pe", hslab_free_prev)
                S.wait("scalar", "dve", cap_decay)
                S.do("scalar", lambda E, xT=xT, tt=tt: E.activation(
                    ptrash32, xT, AF.Square, accum_out=stat("sq", tt)), inc="act")
            cap_p1d, cap_p1a = S.val("dve"), S.val("act")
            S.wait("scalar", "dve", cap_p1d)
            S.do("scalar", lambda E: E.activation(statw("m1"), statw("sm"), AF.Identity,
                                                  scale=1.0 / 2048.0))
            S.do("scalar", lambda E: E.activation(statw("msq"), statw("sq"), AF.Identity,
                                                  scale=1.0 / 2048.0), inc="act")
            ca1 = S.val("act")
            S.wait("vector", "act", ca1)
            S.do("vector", lambda E: E.tensor_mul(statw("mm"), statw("m1"), statw("m1")))
            S.do("vector", lambda E: E.tensor_sub(statw("tmx"), statw("mx"), statw("m1")))
            S.do("vector", lambda E: E.tensor_sub(statw("tmn"), statw("m1"), statw("mn")))
            S.do("vector", lambda E: E.scalar_tensor_tensor(
                statw("vare"), statw("msq"), 1e-5, statw("mm"), ALU.add, ALU.subtract),
                inc="dve")
            cd1 = S.val("dve")
            S.wait("scalar", "dve", cd1)
            S.do("scalar", lambda E: E.activation(statw("lnv"), statw("vare"), AF.Ln))
            S.do("scalar", lambda E: E.activation(statw("lnvb"), statw("vare"), AF.Ln))
            S.do("scalar", lambda E: E.activation(statw("is1"), statw("lnv"), AF.Exp,
                                                  scale=-0.5), inc="act")
            ca2 = S.val("act")
            S.wait("vector", "act", ca2)
            S.do("vector", lambda E: E.tensor_scalar_add(statw("varr"), statw("vare"), -1e-5))
            S.do("vector", lambda E: E.tensor_mul(statw("t1"), statw("m1"), statw("is1")))
            S.do("vector", lambda E: E.tensor_mul(statw("i2"), statw("is1"), statw("is1")))
            S.do("vector", lambda E: E.tensor_max(statw("rd"), statw("tmx"), statw("tmn")))
            S.do("vector", lambda E: E.tensor_mul(statw("v2"), statw("varr"), statw("i2")))
            S.do("vector", lambda E: E.tensor_mul(statw("cl"), statw("is1"), statw("rd")))
            S.do("vector", lambda E: E.tensor_scalar_add(statw("v2e"), statw("v2"), 1e-6))
            S.do("vector", lambda E: E.tensor_scalar_mul(statw("ncl"), statw("cl"), -1.0),
                 inc="dve")
            cd2 = S.val("dve")
            S.wait("scalar", "dve", cd2)
            S.do("scalar", lambda E: E.activation(statw("nb1"), statw("t1"), AF.Identity,
                                                  scale=-1.0))
            S.do("scalar", lambda E: E.activation(statw("lnv2"), statw("v2e"), AF.Ln))
            S.do("scalar", lambda E: E.activation(statw("lnv2b"), statw("v2e"), AF.Ln))
            S.do("scalar", lambda E: E.activation(statw("is2"), statw("lnv2"), AF.Exp,
                                                  scale=-0.5), inc="act")
            ca3 = S.val("act")
            S.wait("vector", "act", ca3)
            S.do("vector", lambda E: E.engine_nop(), inc="dve")
            zxn_free_pe = [0, 0]
            zxn_free_dve = [0, 0]
            for tt in range(NTT):
                zb = tt % 2
                zxn = zxn2[zb]
                xT = xT_v[:, tt]
                S.wait("scalar", "act", ca3)
                S.wait("scalar", "pe", zxn_free_pe[zb])
                S.wait("scalar", "dve", zxn_free_dve[zb])
                S.do("scalar", lambda E, xT=xT, tt=tt, zxn=zxn: E.activation(
                    zxn, xT, AF.Identity, scale=stat("is1", tt), bias=stat("nb1", tt)),
                    inc="act")
                c_xn = S.val("act")
                S.wait("vector", "act", c_xn)
                S.wait("vector", "pe", xhfm_rd_prev)
                S.wait("vector", "gq", xh2w_prev)
                S.do("vector", lambda E, tt=tt, zxn=zxn: E.tensor_scalar(
                    xhat_st[:, tt], zxn, stat("cl", tt), stat("ncl", tt),
                    ALU.min, ALU.max), inc="dve")
                c10 = S.val("dve")
                zxn_free_dve[zb] = c10
                S.wait("gpsimd", "dve", c10)
                dma("gpsimd", xhat_d[tt * 128:(tt + 1) * 128, :], xhat_st[:, tt])
                xh_w = S.val("gq")
                for cc in range(NCH):
                    bank = 6 + (cc % 2)
                    S.wait("tensor", "act", c_xn)
                    S.wait("tensor", pbank_eng[bank], pbank_guard[bank])
                    S.do("tensor", lambda E, cc=cc, bank=bank, zxn=zxn: E.transpose(
                        pb[bank][:, 0:128], zxn[:, cc * 128:(cc + 1) * 128], ident[:]),
                        inc="pe")
                    cpe = S.val("pe")
                    eng, sname = (("vector", "dve") if cc % 2 == 0 else ("scalar", "act"))
                    S.wait(eng, "pe", cpe)
                    S.wait(eng, "dve", cap_car_in)
                    if eng == "vector":
                        S.do(eng, lambda E, cc=cc, tt=tt, bank=bank: E.tensor_copy(
                            xn_fm[:, cc, 1 + tt * 128:1 + (tt + 1) * 128], pb[bank][:, 0:128]),
                            inc=sname)
                    else:
                        S.do(eng, lambda E, cc=cc, tt=tt, bank=bank: E.copy(
                            xn_fm[:, cc, 1 + tt * 128:1 + (tt + 1) * 128], pb[bank][:, 0:128]),
                            inc=sname)
                    pbank_guard[bank] = S.val(sname)
                    pbank_eng[bank] = sname
                zxn_free_pe[zb] = S.val("pe")
            cap_xn_dve, cap_xn_act = S.val("dve"), S.val("act")
            S.wait("vector", "act", cap_xn_act)
            S.do("vector", lambda E: E.tensor_copy(
                xncar[:].rearrange("p (a o) -> p a o", o=1), xn_fm[:, :, TC:TC + 1]), inc="dve")

            lerp_caps = []
            for cc in range(NCH):
                if cc == 0:
                    S.wait("vector", "pe", hslab_free_prev)
                    S.wait("vector", "act", cap_xn_act)
                S.do("vector", lambda E, cc=cc: E.tensor_sub(
                    ptrash32[:, 0:512], xn_fm[:, cc, 1:TC + 1], xn_fm[:, cc, 0:TC]))
                for wi, w in ((0, "k"), (1, "v"), (2, "r")):
                    S.do("vector", lambda E, cc=cc, wi=wi, w=w: E.scalar_tensor_tensor(
                        xi3[:, wi, cc, :], ptrash32[:, 0:512], v_mu[w][:, cc:cc + 1],
                        xn_fm[:, cc, 0:TC], ALU.mult, ALU.add), inc="dve")
                lerp_caps.append(S.val("dve"))
            cap_lerp = S.val("dve")
            if ch == 0 and "xi3" in dumps:
                S.wait("gpsimd", "dve", cap_lerp)
                dma("gpsimd", dump_t["xi3"], xi3[:])
                dbg_gq = S.val("gq")
            if ch == 0 and "xn_fm" in dumps:
                S.wait("gpsimd", "dve", cap_lerp)
                dma("gpsimd", dump_t["xn_fm"], xn_fm[:])
                dbg_gq = S.val("gq")
            if ch == 0 and "stats" in dumps:
                S.wait("gpsimd", "dve", cap_lerp)
                dma("gpsimd", dump_t["stats"], stats[:])
                dbg_gq = S.val("gq")

            if ch == 0:
                emit_attn_prep()
                g_acc_fc = emit_gamma_loads("fc")

            for hb in range(NHB):
                capw = [0, 0]
                for half in range(2):
                    S.wait("sync", "pe", wstr_free[half])
                    S.wait("sync", "gq", cap_wprep["r"])
                    dma("sync", wstr[:, half],
                        wk_src[:, half * 8:(half + 1) * 8, hb, :, :])
                    capw[half] = S.val("sq")
                bk, bv, br = (hb % 2) * 3, (hb % 2) * 3 + 1, (hb % 2) * 3 + 2
                for bank, wi in ((bk, 0), (bv, 1), (br, 2)):
                    for cc in range(NCH):
                        half, c8 = cc // 8, cc % 8
                        if cc == 0:
                            S.wait("tensor", "sq", capw[0])
                            S.wait("tensor", pbank_eng[bank], pbank_guard[bank])
                        if cc == 8:
                            S.wait("tensor", "sq", capw[1])
                        if hb == 0 and wi == 0:
                            S.wait("tensor", "dve", lerp_caps[cc])
                        S.do("tensor", lambda E, bank=bank, half=half, c8=c8, wi=wi, cc=cc: E.matmul(
                            pb[bank][:], wstr[:, half, c8, wi, :], xi3[:, wi, cc, :],
                            start=(cc == 0), stop=(cc == NCH - 1)), inc="pe")
                        if wi == 2 and cc == 7:
                            wstr_free[0] = S.val("pe")
                cap_kvr = S.val("pe")
                wstr_free[1] = cap_kvr
                base = 0 if hb % 2 == 0 else 4096
                ek = bigf[:, base:base + 512]
                al = bigf[:, base + 512:base + 1024]
                AAb = bigf[:, base + 1024:base + 1537]
                BBb = bigf[:, base + 1537:base + 2050]
                num = bigf[:, base + 2050:base + 2562]
                ivd = bigf[:, base + 2562:base + 3074]
                dsx = bigf[:, base + 3074:base + 3586]
                S.wait("scalar", "pe", cap_kvr - 2 * NCH)
                S.wait("scalar", "dve", s_done_prev)
                S.do("scalar", lambda E, ek=ek, bk=bk: E.activation(ek, pb[bk][:], AF.Exp),
                     inc="act")
                c_ek = S.val("act")
                pbank_guard[bk], pbank_eng[bk] = c_ek, "act"
                S.wait("vector", "act", c_ek)
                S.wait("vector", "pe", cap_kvr - NCH)
                S.do("vector", lambda E, al=al, ek=ek, bv=bv: E.tensor_mul(al, ek, pb[bv][:]),
                     inc="dve")
                c_al = S.val("dve")
                pbank_guard[bv], pbank_eng[bv] = c_al, "dve"
                S.wait("scalar", "act", carry_cap_act)
                S.wait("scalar", "dve", cap_decay)
                S.do("scalar", lambda E, AAb=AAb, hb=hb: E.activation(
                    AAb[:, 0:1], aacar[:, hb:hb + 1], AF.Identity))
                S.do("scalar", lambda E, BBb=BBb, hb=hb: E.activation(
                    BBb[:, 0:1], bbcar[:, hb:hb + 1], AF.Identity), inc="act")
                c_exc = S.val("act")
                S.wait("vector", "act", c_exc)
                S.do("vector", lambda E, AAb=AAb, al=al, hb=hb: E.tensor_tensor_scan(
                    AAb[:, 1:1 + SEG], decay[:, hb, :], al[:, 0:SEG],
                    aacar[:, hb:hb + 1], ALU.mult, ALU.add))
                S.do("vector", lambda E, BBb=BBb, ek=ek, hb=hb: E.tensor_tensor_scan(
                    BBb[:, 1:1 + SEG], decay[:, hb, :], ek[:, 0:SEG],
                    bbcar[:, hb:hb + 1], ALU.mult, ALU.add), inc="dve")
                c_s0 = S.val("dve")
                S.wait("scalar", "dve", c_s0)
                S.do("scalar", lambda E, AAb=AAb: E.activation(
                    stats[:, 104:105], AAb[:, SEG:SEG + 1], AF.Identity))
                S.do("scalar", lambda E, BBb=BBb: E.activation(
                    stats[:, 105:106], BBb[:, SEG:SEG + 1], AF.Identity), inc="act")
                c_mid = S.val("act")
                S.wait("vector", "act", c_mid)
                S.do("vector", lambda E, AAb=AAb, al=al, hb=hb: E.tensor_tensor_scan(
                    AAb[:, 1 + SEG:1 + 2 * SEG], decay[:, hb, :], al[:, SEG:2 * SEG],
                    stats[:, 104:105], ALU.mult, ALU.add))
                S.do("vector", lambda E, BBb=BBb, ek=ek, hb=hb: E.tensor_tensor_scan(
                    BBb[:, 1 + SEG:1 + 2 * SEG], decay[:, hb, :], ek[:, SEG:2 * SEG],
                    stats[:, 105:106], ALU.mult, ALU.add), inc="dve")
                c_s1 = S.val("dve")
                S.wait("scalar", "dve", c_s1)
                if ch % 4 == 3:
                    S.do("scalar", lambda E, hb=hb: E.activation(
                        aacar[:, hb:hb + 1], aacar[:, hb:hb + 1], AF.Identity, scale=0.0))
                    S.do("scalar", lambda E, hb=hb: E.activation(
                        bbcar[:, hb:hb + 1], bbcar[:, hb:hb + 1], AF.Identity, scale=0.0),
                        inc="act")
                else:
                    S.do("scalar", lambda E, AAb=AAb, hb=hb: E.activation(
                        aacar[:, hb:hb + 1], AAb[:, TC:TC + 1], AF.Identity))
                    S.do("scalar", lambda E, BBb=BBb, hb=hb: E.activation(
                        bbcar[:, hb:hb + 1], BBb[:, TC:TC + 1], AF.Identity), inc="act")
                carry_cap_act = S.val("act")
                S.do("vector", lambda E, num=num, al=al, AAb=AAb, hb=hb: E.scalar_tensor_tensor(
                    num, al, v_eu[:, hb:hb + 1], AAb[:, 0:TC], ALU.mult, ALU.add))
                S.do("vector", lambda E, dsx=dsx, ek=ek, BBb=BBb, hb=hb: E.scalar_tensor_tensor(
                    dsx, ek, v_eu[:, hb:hb + 1], BBb[:, 0:TC], ALU.mult, ALU.add), inc="dve")
                c_nd = S.val("dve")
                S.wait("scalar", "dve", c_nd)
                S.do("scalar", lambda E, al=al, dsx=dsx: E.activation(al, dsx, AF.Ln))
                S.do("scalar", lambda E, ivd=ivd, al=al: E.activation(ivd, al, AF.Exp, scale=-1.0),
                     inc="act")
                S.wait("scalar", "pe", cap_kvr)
                S.do("scalar", lambda E, dsx=dsx, br=br: E.activation(
                    dsx, pb[br][:], AF.Exp, scale=-1.0), inc="act")
                c_er = S.val("act")
                pbank_guard[br], pbank_eng[br] = c_er, "act"
                S.do("scalar", lambda E, al=al, dsx=dsx: E.activation(al, dsx, AF.Ln, bias=1.0))
                S.do("scalar", lambda E, dsx=dsx, al=al: E.activation(dsx, al, AF.Exp, scale=-1.0),
                     inc="act")
                c_sig = S.val("act")
                S.wait("vector", "act", c_sig)
                S.wait("vector", "pe", hslab_free_prev)
                S.do("vector", lambda E, num=num, ivd=ivd: E.tensor_mul(num, num, ivd))
                S.do("vector", lambda E, num=num, dsx=dsx, hb=hb: E.tensor_mul(
                    z_ch[:, hb, :], dsx, num), inc="dve")
            cap_z_all = S.val("dve")
            if ch == 0 and "z_ch" in dumps:
                S.wait("gpsimd", "dve", cap_z_all)
                dma("gpsimd", dump_t["z_ch"], z_ch[:])
                dbg_gq = S.val("gq")

            for cb in range(4):
                capw = [0, 0]
                for half in range(2):
                    S.wait("sync", "pe", wstr2_free[half])
                    S.wait("sync", "gq", cap_wprep["o"])
                    dma("sync", wstr2[:, half],
                        woT_v[:, half * 8:(half + 1) * 8, cb * 512:(cb + 1) * 512])
                    capw[half] = S.val("sq")
                for tt in range(NTT):
                    bank = (cb % 2) * 4 + tt
                    S.wait("tensor", "dve", cap_z_all)
                    S.wait("tensor", pbank_eng[bank], pbank_guard[bank])
                    S.wait("tensor", "sq", capw[0])
                    for hc in range(16):
                        if hc == 8:
                            S.wait("tensor", "sq", capw[1])
                        S.do("tensor", lambda E, bank=bank, hc=hc, tt=tt: E.matmul(
                            pb[bank][:], z_ch[:, hc, tt * 128:(tt + 1) * 128],
                            wstr2[:, hc // 8, hc % 8, :],
                            start=(hc == 0), stop=(hc == 15)), inc="pe")
                        if tt == NTT - 1 and hc == 7:
                            wstr2_free[0] = S.val("pe")
                    cmm = S.val("pe")
                    S.wait("scalar", "pe", cmm)
                    S.wait("scalar", "dve", s_done_prev)
                    S.do("scalar", lambda E, bank=bank, tt=tt, cb=cb: E.activation(
                        s_tm[:, tt, cb * 512:(cb + 1) * 512], pb[bank][:], AF.Identity),
                        inc="act")
                    pbank_guard[bank], pbank_eng[bank] = S.val("act"), "act"
                wstr2_free[1] = S.val("pe")
            z_free_prev = S.val("pe")
            cap_s_act = S.val("act")
            if ch == 0 and "s_tm" in dumps:
                S.wait("gpsimd", "act", cap_s_act)
                dma("gpsimd", dump_t["s_tm"], s_tm)
                dbg_gq = S.val("gq")
            if ch == 0:
                emit_gamma_final("fc", g_acc_fc)
                emit_tern("fc")
                emit_wT("fc")
                g_acc_pr = emit_gamma_loads("pr")
            S.wait("sync", "gq", xh_w)
            S.wait("sync", "pe", xhfm_rd_prev)
            for dc in range(NCH):
                dmaT(xhat_fm[:, dc, :], xhat_d[:, dc * 128:(dc + 1) * 128])
            cap_xhfm = S.val("sq")

            S.wait("vector", "act", max(ca3, cap_thr.get("fc", 0)))
            S.do("vector", lambda E: E.tensor_scalar_mul(statw("ps1"), statw("is2"),
                                                         gam[:, 0:1]), inc="dve")
            cap_ps1 = S.val("dve")
            for ob in range(NOB):
                capw = [0, 0]
                for half in range(2):
                    S.wait("sync", "pe", wstr2_free[half])
                    S.wait("sync", "gq", max(cap_wT.get("fc", 0), dbg_gq))
                    dma("sync", wstr2[:, half],
                        fcT_v[:, half * 8:(half + 1) * 8, ob * 512:(ob + 1) * 512])
                    capw[half] = S.val("sq")
                sl, col = ob // 4, (ob % 4) * 512
                for tt in range(NTT):
                    bank = (ob % 2) * 4 + tt
                    S.wait("tensor", "sq", max(capw[0], cap_xhfm))
                    S.wait("tensor", pbank_eng[bank], pbank_guard[bank])
                    if ob == 0:
                        S.wait("tensor", "dve", cap_z_all)
                    for dc in range(NCH):
                        if dc == 8:
                            S.wait("tensor", "sq", capw[1])
                        S.do("tensor", lambda E, bank=bank, dc=dc, tt=tt: E.matmul(
                            pb[bank][:], xhat_fm[:, dc, tt * 128:(tt + 1) * 128],
                            wstr2[:, dc // 8, dc % 8, :],
                            start=(dc == 0), stop=(dc == 15)), inc="pe")
                        if tt == NTT - 1 and dc == 7:
                            wstr2_free[0] = S.val("pe")
                    cmm = S.val("pe")
                    S.wait("scalar", "pe", cmm)
                    S.wait("scalar", "dve", cap_ps1)
                    S.do("scalar", lambda E, bank=bank, tt=tt, sl=sl, col=col, ob=ob: E.activation(
                        hslabs[sl][:, tt, col:col + 512], pb[bank][:], AF.Gelu,
                        scale=stat("ps1", tt), accum_out=fst("hsum", tt, ob)), inc="act")
                    c_h = S.val("act")
                    pbank_guard[bank], pbank_eng[bank] = c_h, "act"
                    S.wait("vector", "act", c_h)
                    S.do("vector", lambda E, tt=tt, sl=sl, col=col, ob=ob, bank=bank: E.scalar_tensor_tensor(
                        pb[bank][:], hslabs[sl][:, tt, col:col + 512], 1.0,
                        hslabs[sl][:, tt, col:col + 512], ALU.mult, ALU.mult,
                        accum_out=fst("hsq", tt, ob)))
                    S.do("vector", lambda E, tt=tt, sl=sl, col=col, ob=ob: E.tensor_reduce(
                        fst("hmx", tt, ob), hslabs[sl][:, tt, col:col + 512], AX, ALU.max))
                    S.do("vector", lambda E, tt=tt, sl=sl, col=col, ob=ob: E.tensor_reduce(
                        fst("hmn", tt, ob), hslabs[sl][:, tt, col:col + 512], AX, ALU.min),
                        inc="dve")
                    pbank_guard[bank] = S.val("dve")
                    pbank_eng[bank] = "dve"
                wstr2_free[1] = S.val("pe")
            xhfm_rd_prev = S.val("pe")
            cap_f1_dve = S.val("dve")
            if ch == 0:
                emit_gamma_final("pr", g_acc_pr)
                emit_tern("pr")
                emit_wT("pr")

            def fw(nm):
                b = FB[nm]
                return fstat[:, :, b:b + 16]

            def fwf(i):
                return fstat[:, :, 64 + i:65 + i]

            S.wait("vector", "dve", cap_f1_dve)
            S.do("vector", lambda E: E.tensor_reduce(fwf(12), fw("hsum"), AX, ALU.add))
            S.do("vector", lambda E: E.tensor_reduce(fwf(13), fw("hsq"), AX, ALU.add))
            S.do("vector", lambda E: E.tensor_reduce(fwf(14), fw("hmx"), AX, ALU.max))
            S.do("vector", lambda E: E.tensor_reduce(fwf(15), fw("hmn"), AX, ALU.min),
                 inc="dve")
            cf1 = S.val("dve")
            S.wait("scalar", "dve", cf1)
            S.do("scalar", lambda E: E.activation(fwf(0), fwf(12), AF.Identity,
                                                  scale=1.0 / 8192.0))
            S.do("scalar", lambda E: E.activation(fwf(6), fwf(13), AF.Identity,
                                                  scale=1.0 / 8192.0), inc="act")
            cf2 = S.val("act")
            S.wait("vector", "act", cf2)
            S.do("vector", lambda E: E.tensor_mul(fwf(11), fwf(0), fwf(0)))
            S.do("vector", lambda E: E.tensor_sub(fwf(7), fwf(14), fwf(0)))
            S.do("vector", lambda E: E.tensor_sub(fwf(8), fwf(0), fwf(15)))
            S.do("vector", lambda E: E.scalar_tensor_tensor(
                fwf(9), fwf(6), 1e-6, fwf(11), ALU.add, ALU.subtract))
            S.do("vector", lambda E: E.tensor_max(fwf(7), fwf(7), fwf(8)), inc="dve")
            cf3 = S.val("dve")
            S.wait("scalar", "dve", cf3)
            S.do("scalar", lambda E: E.activation(fwf(10), fwf(9), AF.Ln))
            for tt in range(NTT):
                S.do("scalar", lambda E, tt=tt: E.activation(
                    ffin(2, tt), ffin(7, tt), AF.Identity, bias=ffin(0, tt)))
                S.do("scalar", lambda E, tt=tt: E.activation(
                    ffin(3, tt), ffin(7, tt), AF.Identity, scale=-1.0, bias=ffin(0, tt)))
                S.do("scalar", lambda E, tt=tt: E.activation(
                    ffin(4, tt), ffin(0, tt), AF.Identity, scale=-1.0))
            S.do("scalar", lambda E: E.activation(fwf(1), fwf(10), AF.Exp, scale=-0.5),
                 inc="act")
            cf4 = S.val("act")
            S.wait("vector", "act", max(cf4, cap_thr.get("pr", 0)))
            S.do("vector", lambda E: E.tensor_scalar_mul(fwf(5), fwf(1), gam[:, 1:2]),
                 inc="dve")
            cap_f2_dve = S.val("dve")
            S.wait("scalar", "dve", cap_f2_dve)
            S.do("scalar", lambda E: E.activation(fwf(10), fwf(9), AF.Ln), inc="act")
            cap_f2_act = S.val("act")
            if ch == 0 and "fstat" in dumps:
                S.wait("gpsimd", "act", cap_f2_act)
                dma("gpsimd", dump_t["fstat"], fstat.ap().rearrange("p a b -> p (a b)"))
                dbg_gq = S.val("gq")
            if ch == 0 and "h0" in dumps:
                S.wait("gpsimd", "act", cap_f2_act)
                dma("gpsimd", dump_t["h0"], hslabs[0])
                dbg_gq = S.val("gq")

            xh_flat = xh_sl.ap()
            zt = [xh_flat[:, 0:2048], xh_flat[:, 2048:4096],
                  xh_flat[:, 4096:6144], xh_flat[:, 6144:8192]]
            zt_free_gq = [0, 0]
            it = 0
            slab_fm_cap = [0] * 4
            for sl in range(4):
                for tt in range(NTT):
                    p = it % 2
                    S.wait("vector", "act", cap_f2_act)
                    S.wait("vector", "pe", xhfm_rd_prev)
                    S.wait("vector", "gq", max(zt_free_gq[p], dbg_gq))
                    S.do("vector", lambda E, sl=sl, tt=tt, p=p: E.tensor_scalar(
                        zt[p * 2], hslabs[sl][:, tt, :], ffin(2, tt), ffin(3, tt),
                        ALU.min, ALU.max), inc="dve")
                    c_cl = S.val("dve")
                    S.wait("scalar", "dve", c_cl)
                    S.do("scalar", lambda E, tt=tt, p=p: E.activation(
                        zt[p * 2 + 1], zt[p * 2], AF.Identity, bias=ffin(4, tt)), inc="act")
                    c_b = S.val("act")
                    S.wait("gpsimd", "act", c_b)
                    dma("gpsimd", xh2_d[tt * 128:(tt + 1) * 128, sl * 2048:(sl + 1) * 2048],
                        zt[p * 2 + 1])
                    zt_free_gq[p] = S.val("gq")
                    it += 1
                S.wait("sync", "gq", S.val("gq"))
                S.wait("sync", "pe", xhfm_rd_prev)
                for j in range(16):
                    oc = sl * 16 + j
                    dmaT(xh2fm[sl][:, j, :], xh2_d[:, oc * 128:(oc + 1) * 128])
                slab_fm_cap[sl] = S.val("sq")
            cap_xh2w = S.val("gq")
            xh2w_prev = cap_xh2w
            cap_xh2fm = S.val("sq")

            og_free = [0, 0]
            xj_free = [0, 0]
            g_free = [0, 0]
            def f4_load(d2q, g8):
                bidx = g8 % 2
                S.wait("sync", "pe", wstr2_free[bidx])
                S.wait("sync", "gq", cap_wT.get("pr", 0))
                dma("sync", wstr2[:, bidx],
                    prT_v[:, g8 * 8:(g8 + 1) * 8, d2q * 512:(d2q + 1) * 512])
                return S.val("sq")

            for d2q in range(4):
                banks = [(d2q % 2) * 4 + tt for tt in range(NTT)]
                for tt in range(NTT):
                    S.wait("tensor", pbank_eng[banks[tt]], pbank_guard[banks[tt]])
                grp_cap = [0] * 8
                grp_cap[0] = f4_load(d2q, 0)
                grp_cap[1] = f4_load(d2q, 1)
                for g8 in range(8):
                    bidx = g8 % 2
                    S.wait("tensor", "sq", max(grp_cap[g8], slab_fm_cap[g8 // 2]))
                    for tt in range(NTT):
                        for o8 in range(8):
                            oc = g8 * 8 + o8
                            sl, j = oc // 16, oc % 16
                            S.do("tensor", lambda E, bank=banks[tt], sl=sl, j=j, tt=tt,
                                 bidx=bidx, o8=o8, oc=oc: E.matmul(
                                pb[bank][:], xh2fm[sl][:, j, tt * 128:(tt + 1) * 128],
                                wstr2[:, bidx, o8, :],
                                start=(oc == 0), stop=(oc == NOC - 1)), inc="pe")
                    wstr2_free[bidx] = S.val("pe")
                    if g8 + 2 < 8:
                        grp_cap[g8 + 2] = f4_load(d2q, g8 + 2)
                cap_q = S.val("pe")
                for tt in range(NTT):
                    gb = tt % 2
                    S.wait("scalar", "pe", cap_q)
                    S.wait("scalar", "dve", max(cap_f2_dve, g_free[gb]))
                    S.do("scalar", lambda E, bank=banks[tt], tt=tt, gb=gb: E.activation(
                        gbuf[:, gb], pb[bank][:], AF.Silu, scale=ffin(5, tt)), inc="act")
                    c_g = S.val("act")
                    pbank_guard[banks[tt]], pbank_eng[banks[tt]] = c_g, "act"
                    xb = tt % 2
                    S.wait("gpsimd", "dve", xj_free[xb])
                    if ch == 0:
                        S.wait("gpsimd", "gq", cap_wT.get("pr", 0))
                    dma("gpsimd", xjit[:, xb],
                        x_in[seq, t0 + tt * 128:t0 + (tt + 1) * 128,
                             d2q * 512:(d2q + 1) * 512])
                    c_xj = S.val("gq")
                    ob_ = tt % 2
                    S.wait("vector", "act", max(c_g, cap_s_act))
                    S.wait("vector", "gq", max(c_xj, og_free[ob_]))
                    S.do("vector", lambda E, gb=gb, tt=tt, d2q=d2q, ob_=ob_: E.tensor_mul(
                        outst[:, ob_], gbuf[:, gb], s_tm[:, tt, d2q * 512:(d2q + 1) * 512]))
                    S.do("vector", lambda E, ob_=ob_, xb=xb: E.tensor_add(
                        outst[:, ob_], outst[:, ob_], xjit[:, xb]), inc="dve")
                    c_cmb = S.val("dve")
                    g_free[gb] = c_cmb
                    xj_free[xb] = c_cmb
                    S.wait("gpsimd", "dve", c_cmb)
                    dma("gpsimd",
                        out_t[seq, t0 + tt * 128:t0 + (tt + 1) * 128,
                              d2q * 512:(d2q + 1) * 512], outst[:, ob_])
                    og_free[ob_] = S.val("gq")
            hslab_free_prev = S.val("pe")
            s_done_prev = S.val("dve")

        S.wait("gpsimd", "gq", S.val("gq"))
        S.wait("sync", "sq", S.val("sq"))

        S.verify()
        blk = e(nc.Block())
        for name in ENGINES:
            def make(nm):
                def body(E):
                    S.replay(nm, E, H)
                return body
            getattr(blk, name)(make(name))

    return nc, dump_t


def kernel(L0_attn_x, L0_attn_mu_r, L0_attn_mu_k, L0_attn_mu_v, L0_attn_wkv_w,
           L0_attn_wkv_u, L0_attn_Wr, L0_attn_Wk, L0_attn_Wv, L0_attn_Wo,
           L0_ffn_w_fc, L0_ffn_w_proj):
    nc, _ = build()
    ident = np.eye(128, dtype=np.float32)
    in_maps = []
    for core in range(8):
        in_maps.append(dict(
            x=np.ascontiguousarray(L0_attn_x[2 * core:2 * core + 2]),
            Wr=L0_attn_Wr, Wk=L0_attn_Wk, Wv=L0_attn_Wv, Wo=L0_attn_Wo,
            Wfc=L0_ffn_w_fc, Wpr=L0_ffn_w_proj,
            mur=L0_attn_mu_r, muk=L0_attn_mu_k, muv=L0_attn_mu_v,
            wkvw=L0_attn_wkv_w, wkvu=L0_attn_wkv_u, identd=ident,
        ))
    res = run_bass_kernel_spmd(nc, in_maps, core_ids=list(range(8)))
    return np.concatenate([r["out"] for r in res.results], axis=0)
